# version 33
# baseline (speedup 1.0000x reference)
import math
import numpy as np
import concourse.bass as bass
import concourse.mybir as mybir
from concourse.bass_utils import run_bass_kernel_spmd

F32 = mybir.dt.float32
BF16 = mybir.dt.bfloat16
AF = mybir.ActivationFunctionType
ALU = mybir.AluOpType
AX = mybir.AxisListType

D = 2048
CH = 16
T = 1026
NW = 342
NN = 3
SL = 16
C_SHIFT = float(-math.log(float(np.float32(1e-8))))
EPS_LN = 1e-5
FFN_EPS = 1e-10

MIX_W = ["wr", "wk", "wv", "wo", "cwk", "cwr", "cwv"]
VECS = ["g1", "b1", "g2", "b2", "u", "mur", "muk", "muv", "cmur", "cmuk", "fbias"]


def _f32(x):
    return np.float32(x)


class Ctr:

    def __init__(self, nc, name):
        self.h = nc.alloc_semaphore(name)
        self.n = 0

    def bump(self, k=1):
        self.n += k
        return self.n

    def ticket(self):
        return (self.h, self.n)


class PairCtr:

    def __init__(self, nc, name, n=2):
        self.n = n
        self.h = [nc.alloc_semaphore(f"{name}{i}") for i in range(n)]
        self.cnt = [0] * n

    def issue(self, idx):
        p = idx % self.n
        self.cnt[p] += 16
        return (self.h[p], self.cnt[p])


def build_program(trunc=None):
    nc = bass.Bass("TRN2", target_bir_lowering=False)

    xT = nc.dram_tensor("xT", [D, T], F32, kind="ExternalInput").ap()
    xTr = xT.rearrange("(k p) t -> k p t", p=128)
    wt = {w: nc.dram_tensor(f"wt_{w}", [SL, 128, CH, 128], F32, kind="ExternalInput").ap()
          for w in MIX_W}
    wtf = nc.dram_tensor("wt_ffn", [SL, 128, CH, 128], F32, kind="ExternalInput").ap()
    vec_d = {v: nc.dram_tensor(f"v_{v}", [128, CH], F32, kind="ExternalInput").ap()
             for v in VECS}
    mask_d = nc.dram_tensor("mask", [128, 2], F32, kind="ExternalInput").ap()
    outT = nc.dram_tensor("outT", [CH, 128, 1024], F32, kind="ExternalOutput").ap()
    x1s = nc.dram_tensor("x1s", [CH, 128, T], F32, kind="Internal").ap()
    x2s = nc.dram_tensor("x2s", [CH, 128, T], F32, kind="Internal").ap()
    wts = nc.dram_tensor("wts", [SL, 128, CH, 128], BF16, kind="Internal").ap()

    B0 = (nc.sbuf_base + 31) // 32 * 32
    off = B0

    def at(name, shape, dtype, o):
        return nc.alloc_sbuf_tensor_at(name, shape, dtype, offset=o)

    offs = {}

    def alloc(name, shape, dtype):
        nonlocal off
        nbytes = int(np.prod(shape[1:])) * mybir.dt.size(dtype)
        offs[name] = off
        t = at(name, shape, dtype, off)
        off += (nbytes + 31) // 32 * 32
        return t

    m1 = alloc("m1", [128, CH, T], BF16)
    m2 = alloc("m2", [128, CH, T], BF16)
    m3 = alloc("m3", [128, CH, T], BF16)
    m4 = alloc("m4", [128, CH, T], BF16)
    x3f = at("x3f", [128, CH, T], F32, offs["m1"])
    w_t = at("w_t", [128, 2, CH, 128], BF16, offs["m4"])
    xc = alloc("xc", [128, 2, T], F32)
    hc = alloc("hc", [128, 4, T], F32)
    hcpre = at("hcpre", [128, 2, 8, 128], F32, offs["hc"])
    wfst = at("wfst", [128, 2, CH, 128], F32, offs["m4"] + 8192)
    S = alloc("S", [128, T], F32)
    S2 = alloc("S2", [128, T], F32)
    alpha = alloc("alpha", [128, T], F32)
    beta = alloc("beta", [128, T], F32)
    sq = [at("sq0", [128, T], F32, offs["alpha"]),
          at("sq1", [128, T], F32, offs["beta"])]
    r_sl = alloc("r_sl", [128, 2, T], F32)
    tern_scr = at("tern_scr", [128, 2048], F32, offs["r_sl"])
    s_sl = alloc("s_sl", [128, 2, T], F32)
    wsl = alloc("wsl", [128, 4, CH, 128], BF16)
    rl_scr = alloc("rl_scr", [128, 2, NW], F32)
    vec = {v: alloc(f"t_{v}", [128, CH], F32) for v in VECS}
    ush = alloc("ush", [128, CH], F32)
    ones = alloc("ones", [128, 128], F32)
    mask = alloc("maskt", [128, 2], F32)
    theta = alloc("theta", [128, 1], F32)
    scale_t = alloc("scale_t", [128, 1], F32)
    absacc = alloc("absacc", [128, 32], F32)
    absacc1 = alloc("absacc1", [128, 1], F32)
    assert off <= nc.sbuf_top - 32, f"SBUF overflow: {off} > {nc.sbuf_top}"

    ps = [nc.alloc_psum_tensor(f"ps{i}", [128, NW], F32) for i in range(6)]

    s_x = Ctr(nc, "s_x")
    s_mm = Ctr(nc, "s_mm")
    e_act = Ctr(nc, "e_act")
    e_dve = Ctr(nc, "e_dve")
    s_v = Ctr(nc, "s_v")
    s_a = Ctr(nc, "s_a")
    s_g = Ctr(nc, "s_g")
    p_w = PairCtr(nc, "p_w", 4)
    p_xc = PairCtr(nc, "p_xc")
    p_sl = PairCtr(nc, "p_sl")
    p_pre = PairCtr(nc, "p_pre")
    p_wf = PairCtr(nc, "p_wf")
    p_s1 = PairCtr(nc, "p_s1")
    p_s2 = PairCtr(nc, "p_s2")
    p_o = PairCtr(nc, "p_o", 4)
    p_wt = PairCtr(nc, "p_wt")

    E = {k: [] for k in ("sync", "scalar", "vector", "tensor", "gpsimd")}
    nops = [0]

    def op(eng, fn):
        nops[0] += 1
        if trunc is None or nops[0] <= trunc:
            E[eng].append(fn)

    def wait(eng, ctr, val):
        if val > 0:
            op(eng, lambda e, h=ctr.h, v=val: e.wait_ge(h, v))

    def waitt(eng, ticket):
        if ticket is None:
            return
        h, v = ticket
        if v > 0:
            op(eng, lambda e, h=h, v=v: e.wait_ge(h, v))

    def drain(eng):
        op(eng, lambda e: e.drain())

    units = [("ln", 1)]
    units += [("mm", ("rkv", s, w)) for s in range(SL) for w in ("r", "k", "v")]
    units += [("theta", None)]
    units += [("mm", ("o", s, "o")) for s in range(SL)]
    units += [("ln", 2)]
    units += [("mm", ("ck", s, "ck")) for s in range(SL)]
    units += [("mm", ("crv", s, w)) for s in range(SL) for w in ("cr", "cv")]
    units += [("ln", 3)]
    units += [("mm2", ("f8", s, 0)) for s in range(SL)]
    units += [("lnh", 0)]
    units += [("mm2", ("f8", s, 1)) for s in range(SL)]
    units += [("lnh", 1)]

    mm_cum = []
    mm_n = 0
    mm_idx_of = {}
    for ui, (kind, info) in enumerate(units):
        if kind == "mm":
            mm_idx_of[info] = ui
            mm_n += 3
        elif kind == "mm2":
            mm_idx_of[info] = ui
            mm_n += 2
        elif kind == "ln":
            mm_n += 6
        elif kind == "lnh":
            mm_n += 4
        else:
            mm_n += 1
        mm_cum.append(mm_n)

    UNIT_TH = 1 + 48
    UNIT_LN2 = UNIT_TH + 1 + 16
    UNIT_LN3 = UNIT_LN2 + 1 + 16 + 32
    UNIT_LNH0 = UNIT_LN3 + 1 + 16
    UNIT_LNH1 = UNIT_LNH0 + 1 + 16
    assert units[UNIT_LN2] == ("ln", 2) and units[UNIT_TH] == ("theta", None)
    assert units[UNIT_LN3] == ("ln", 3)
    assert units[UNIT_LNH0] == ("lnh", 0) and units[UNIT_LNH1] == ("lnh", 1)

    last_ev_for_set = [None, None]
    mm_parity = [0]

    wload_info = []
    for s in range(SL):
        for w, nm in (("r", "wr"), ("k", "wk"), ("v", "wv")):
            wload_info.append((("rkv", s, w), wt[nm][s]))
    for s in range(SL):
        wload_info.append((("o", s, "o"), wt["wo"][s]))
    for s in range(SL):
        wload_info.append((("ck", s, "ck"), wt["cwk"][s]))
    for s in range(SL):
        for w, nm in (("cr", "cwr"), ("cv", "cwv")):
            wload_info.append((("crv", s, w), wt[nm][s]))
    for h in (0, 1):
        for s in range(SL):
            wload_info.append((("f8", s, h), wts[s]))
    extra_load_wait = {}
    w_ticket = {}
    w_pumped = [0]
    WLOOK = 4

    def pump_loads(until):
        for i in range(w_pumped[0], min(until, len(wload_info))):
            key, srcw = wload_info[i]
            if i >= WLOOK:
                prev_ui = mm_idx_of[wload_info[i - WLOOK][0]]
                wait("gpsimd", s_mm, mm_cum[prev_ui])
            if key in extra_load_wait:
                waitt("gpsimd", extra_load_wait[key])
            b = i % 4
            tk = p_w.issue(i)
            w_ticket[key] = (tk, b)
            op("gpsimd", lambda g, srcw=srcw, b=b, h=tk[0]: g.dma_start(
                out=wsl[:, b, :, :], in_=srcw).then_inc(h, 16))
        w_pumped[0] = max(w_pumped[0], min(until, len(wload_info)))

    load_idx_of = {key: i for i, (key, _) in enumerate(wload_info)}

    COLS3 = [(0, NW), (NW, NW), (2 * NW, NW)]
    COLS_H = [[(0, NW), (NW, 171)], [(513, NW), (513 + NW, 171)]]

    def emit_mm_unit(key, rhs_t, lhs_t=None, lhs_b=None, pre_waits=(), cols=None):
        if cols is None:
            cols = COLS3
        ui = mm_idx_of[key]
        st = mm_parity[0] % 2
        mm_parity[0] += 1
        waitt("tensor", last_ev_for_set[st])
        if lhs_t is None:
            pump_loads(load_idx_of[key] + 1 + WLOOK)
            tk, b = w_ticket[key]
            waitt("tensor", tk)
            lhs_t, lhs_b = wsl, b
        for tkt in pre_waits:
            waitt("tensor", tkt)
        base = mm_cum[ui] - len(cols)

        def f(t, st=st, b=lhs_b, lhs_t=lhs_t, rhs_t=rhs_t, cols=cols):
            for k in range(CH):
                for n, (c0, w) in enumerate(cols):
                    ins = t.matmul(ps[st * 3 + n][:, 0:w], lhs_t[:, b, k, :],
                                   rhs_t[:, k, c0:c0 + w],
                                   start=(k == 0), stop=(k == CH - 1))
                    if k == CH - 1:
                        ins.then_inc(s_mm.h, 1)
        op("tensor", f)
        return st, base

    def set_unit_evac(st, ticket):
        last_ev_for_set[st] = ticket

    def emit_ln_bcast(ui, fold_ticket):
        waitt("tensor", last_ev_for_set[0])
        waitt("tensor", last_ev_for_set[1])
        waitt("tensor", fold_ticket)
        base = mm_cum[ui] - 6

        def f(t):
            for i, srcT in enumerate((S, S2)):
                for n in range(NN):
                    t.matmul(ps[i * 3 + n][:, :], ones[:, :],
                             srcT[:, n * NW:(n + 1) * NW],
                             start=True, stop=True).then_inc(s_mm.h, 1)
        op("tensor", f)
        return base

    def emit_stats_math(ui, bc_base):
        for n in range(NN):
            wait("scalar", s_mm, bc_base + n + 1)
            wait("vector", s_mm, bc_base + 3 + n + 1)
            nw = slice(n * NW, (n + 1) * NW)
            drain("scalar")
            tka1 = s_a.bump()
            op("scalar", lambda sc, nw=nw, n=n: sc.activation(
                beta[:, nw], ps[n][:, :], AF.Square).then_inc(s_a.h, 1))
            wait("vector", s_a, tka1)
            drain("vector")
            tkv = s_v.bump()
            op("vector", lambda v, nw=nw, n=n: v.scalar_tensor_tensor(
                alpha[:, nw], beta[:, nw], -1.0 / 2048.0, ps[3 + n][:, :],
                ALU.mult, ALU.add).then_inc(s_v.h, 1))
            wait("scalar", s_v, tkv)
            op("scalar", lambda sc, nw=nw: sc.activation(
                alpha[:, nw], alpha[:, nw], AF.Ln,
                bias=_f32(EPS_LN), scale=_f32(1.0 / 2048.0)))
            drain("scalar")
            tka = s_a.bump()
            op("scalar", lambda sc, nw=nw: sc.activation(
                alpha[:, nw], alpha[:, nw], AF.Exp,
                bias=_f32(0.0), scale=_f32(-0.5)).then_inc(s_a.h, 1))
            wait("vector", s_a, tka)
            drain("vector")
            e_dve.bump()
            op("vector", lambda v, nw=nw, n=n: v.scalar_tensor_tensor(
                beta[:, nw], ps[n][:, :], -1.0 / 2048.0, alpha[:, nw],
                ALU.mult, ALU.mult).then_inc(e_dve.h, 1))
        tk = e_dve.ticket()
        set_unit_evac(0, tk)
        set_unit_evac(1, tk)
        return tk

    for vname in VECS:
        s_x.bump(16)
        op("sync", lambda e, d=vec_d[vname], t=vec[vname]: e.dma_start(
            out=t[:, :], in_=d).then_inc(s_x.h, 16))
    s_x.bump(16)
    op("sync", lambda e: e.dma_start(out=mask[:, :], in_=mask_d).then_inc(s_x.h, 16))
    vecs_done = s_x.n

    M_ones = s_v.bump()
    op("vector", lambda v: v.memset(ones[:, :], 1.0).then_inc(s_v.h, 1))
    wait("vector", s_x, vecs_done)
    M_ush = s_v.bump()
    op("vector", lambda v: v.tensor_scalar_add(
        ush[:, :], vec["u"][:, :], C_SHIFT).then_inc(s_v.h, 1))

    pump_loads(WLOOK)

    def emit_fold_chunk(idx, src_ap, sq_b, eng="vector", ctr=None):
        if ctr is None:
            ctr = s_v
        drain(eng)
        if idx == 0:
            op(eng, lambda v, src_ap=src_ap: v.tensor_copy(S[:, :], src_ap))
            ctr.bump()
            op(eng, lambda v, sq_b=sq_b, h=ctr.h: v.tensor_copy(
                S2[:, :], sq[sq_b][:, :]).then_inc(h, 1))
        else:
            op(eng, lambda v, src_ap=src_ap: v.tensor_add(S[:, :], S[:, :], src_ap))
            ctr.bump()
            op(eng, lambda v, sq_b=sq_b, h=ctr.h: v.tensor_add(
                S2[:, :], S2[:, :], sq[sq_b][:, :]).then_inc(h, 1))
        return ctr.ticket()

    xc_idx = [0]

    def xc_dma(src_ap, width=T):
        i = xc_idx[0]
        xc_idx[0] += 1
        b = i % 2
        tk = p_xc.issue(i)
        op("sync", lambda e, src_ap=src_ap, b=b, h=tk[0], w=width: e.dma_start(
            out=xc[:, b, 0:w], in_=src_ap).then_inc(h, 16))
        return tk, b

    fold_v, fold_a = [], []
    for i in range(CH):
        if i >= 2:
            waitt("sync", fold_v[i - 2])
            wait("sync", s_a, fold_a[i - 2])
        tkx, b = xc_dma(xTr[i])
        waitt("scalar", tkx)
        if i >= 2:
            waitt("scalar", fold_v[i - 2])
        fold_a.append(s_a.bump())
        op("scalar", lambda sc, b=b: sc.activation(
            sq[b][:, :], xc[:, b, :], AF.Square).then_inc(s_a.h, 1))
        waitt("vector", tkx)
        wait("vector", s_a, fold_a[i])
        fold_v.append(emit_fold_chunk(i, xc[:, b, :], b))

    bc1 = emit_ln_bcast(0, fold_v[-1])
    ab1_tk = emit_stats_math(0, bc1)

    HB = 692

    def emit_apply_lerps(chunk_src, g_t, b_t, lerps, gp_pre=()):
        h_tk = [None] * CH
        dv_tk = [None] * CH
        gp_tk = [None] * CH
        wait("vector", s_x, vecs_done)
        wait("gpsimd", s_x, vecs_done)
        wait("scalar", s_x, vecs_done)
        for t in gp_pre:
            waitt("gpsimd", t)
        for i in range(CH):
            if i >= 2:
                wait("sync", s_g, gp_tk[i - 2])
            tkx, b = chunk_src(i)
            bh = i % 4
            waitt("gpsimd", tkx)
            if i >= 4:
                wait("gpsimd", s_v, dv_tk[i - 4])
            drain("gpsimd")
            op("gpsimd", lambda g, b=b, bh=bh: g.tensor_mul(
                hc[:, bh, :], xc[:, b, :], alpha[:, :]))
            drain("gpsimd")
            qb = s_g.bump()
            op("gpsimd", lambda g, b=b, bh=bh: g.tensor_add(
                hc[:, bh, :], hc[:, bh, :], beta[:, :]).then_inc(s_g.h, 1))
            wait("scalar", s_g, qb)
            op("scalar", lambda sc, bh=bh, i=i: sc.activation(
                hc[:, bh, :], hc[:, bh, :], AF.Identity,
                bias=b_t[:, i:i + 1], scale=g_t[:, i:i + 1]))
            drain("scalar")
            h_tk[i] = s_a.bump()
            op("scalar", lambda sc, bh=bh: sc.activation(
                hc[:, bh, 0:2], hc[:, bh, 0:2], AF.Identity,
                scale=mask[:, 0:1]).then_inc(s_a.h, 1))
            gp_tk[i] = qb
            wait("vector", s_a, h_tk[i])
            drain("vector")
            op("vector", lambda v, b=b, bh=bh: v.tensor_sub(
                s_sl[:, b, 1:T], hc[:, bh, 1:T], hc[:, bh, 0:T - 1]))
            drain("vector")
            for j, (mu_t, dst) in enumerate(lerps):
                op("vector", lambda v, b=b, bh=bh, i=i, mu_t=mu_t, dst=dst:
                   v.scalar_tensor_tensor(
                       dst[:, i, 1:T], s_sl[:, b, 1:T], mu_t[:, i:i + 1],
                       hc[:, bh, 0:T - 1], ALU.mult, ALU.add))
                dv_tk[i] = s_v.bump()
                op("vector", lambda v, bh=bh, i=i, mu_t=mu_t, dst=dst:
                   v.tensor_single_scalar(
                       dst[:, i, 0:1], hc[:, bh, 0:1], mu_t[:, i:i + 1],
                       ALU.mult).then_inc(s_v.h, 1))
        return (s_v.h, dv_tk[CH - 1]), (s_g.h, gp_tk[CH - 1])

    waitt("sync", fold_v[CH - 1])
    wait("sync", s_a, fold_a[CH - 1])
    M_lerp1_v, M_lerp1_g = emit_apply_lerps(
        lambda i: xc_dma(xTr[i]), vec["g1"], vec["b1"],
        [(vec["mur"], m1), (vec["muk"], m2), (vec["muv"], m3)],
        gp_pre=[ab1_tk])

    waitt("sync", M_lerp1_v)
    waitt("sync", M_lerp1_g)
    pre_v = [None] * 32

    def prepass_dma(j):
        if j >= 32:
            return
        if j >= 2:
            wait("sync", s_v, pre_v[j - 2])
        sj, hj = j // 2, j % 2
        tkpre = p_pre.issue(j)
        pre_dma_tk[j] = tkpre
        op("sync", lambda e, sj=sj, hj=hj, b=j % 2, h=tkpre[0]: e.dma_start(
            out=hcpre[:, b, :, :], in_=wtf[sj, :, 8 * hj:8 * hj + 8, :]
        ).then_inc(h, 16))

    def prepass_reduce(j):
        if j >= 32:
            return
        waitt("vector", pre_dma_tk[j])
        drain("vector")
        pre_v[j] = s_v.bump()
        op("vector", lambda v, j=j, b=j % 2: v.reduce_sum(
            absacc[:, j:j + 1],
            hcpre[:, b, :, :].rearrange("p a q -> p (a q)"), AX.X,
            apply_absolute_value=True).then_inc(s_v.h, 1))

    pre_dma_tk = [None] * 32
    prepass_dma(0)
    prepass_dma(1)

    waitt("tensor", M_lerp1_v)
    waitt("tensor", M_lerp1_g)
    wait("scalar", s_v, M_ush)
    a_ev_tk = [None] * SL
    for s in range(SL):
        b = s % 2
        st, base = emit_mm_unit(("rkv", s, "r"), m1)
        if s >= 2:
            waitt("scalar", a_ev_tk[s - 2])
        for n in range(NN):
            wait("scalar", s_mm, base + n + 1)
            e_act.bump()
            op("scalar", lambda sc, st=st, n=n, b=b: sc.activation(
                r_sl[:, b, n * NW:(n + 1) * NW], ps[st * 3 + n][:, :],
                AF.Sigmoid).then_inc(e_act.h, 1))
        set_unit_evac(st, e_act.ticket())
        st, base = emit_mm_unit(("rkv", s, "k"), m2)
        for n in range(NN):
            wait("scalar", s_mm, base + n + 1)
            e_act.bump()
            op("scalar", lambda sc, st=st, n=n, b=b, s=s: sc.activation(
                s_sl[:, b, n * NW:(n + 1) * NW], ps[st * 3 + n][:, :],
                AF.Sigmoid, bias=ush[:, s:s + 1]).then_inc(e_act.h, 1))
        set_unit_evac(st, e_act.ticket())
        rs_tk = e_act.ticket()
        st, base = emit_mm_unit(("rkv", s, "v"), m3)
        waitt("vector", rs_tk)
        for n in range(NN):
            wait("vector", s_mm, base + n + 1)
            nw = slice(n * NW, (n + 1) * NW)
            drain("vector")
            op("vector", lambda v, st=st, n=n, b=b, nw=nw: v.tensor_mul(
                rl_scr[:, n % 2, :], ps[st * 3 + n][:, :], s_sl[:, b, nw]))
            drain("vector")
            e_dve.bump()
            op("vector", lambda v, n=n, b=b, s=s, nw=nw: v.tensor_mul(
                m4[:, s, nw], rl_scr[:, n % 2, :],
                r_sl[:, b, nw]).then_inc(e_dve.h, 1))
        set_unit_evac(st, e_dve.ticket())
        a_ev_tk[s] = e_dve.ticket()
        prepass_reduce(2 * s)
        prepass_reduce(2 * s + 1)
        prepass_dma(2 * s + 2)
        prepass_dma(2 * s + 3)

    drain("vector")
    M_absfold = s_v.bump()
    op("vector", lambda v: v.reduce_sum(
        absacc1[:, :], absacc[:, :], AX.X,
        apply_absolute_value=False).then_inc(s_v.h, 1))

    waitt("tensor", last_ev_for_set[0])
    wait("tensor", s_v, M_absfold)
    th_base = mm_cum[UNIT_TH] - 1
    op("tensor", lambda t: t.matmul(
        ps[0][:, 0:1], ones[:, :], absacc1[:, :],
        start=True, stop=True).then_inc(s_mm.h, 1))
    wait("vector", s_mm, th_base + 1)
    drain("vector")
    op("vector", lambda v: v.tensor_scalar_mul(
        scale_t[:, :], ps[0][:, 0:1], 1.0 / 4194304.0))
    drain("vector")
    e_dve.bump()
    op("vector", lambda v: v.tensor_scalar(
        theta[:, :], scale_t[:, :], 0.5, 0.5 * FFN_EPS, ALU.mult, ALU.add
    ).then_inc(e_dve.h, 1))
    set_unit_evac(0, e_dve.ticket())
    M_theta_v = e_dve.ticket()

    waitt("tensor", a_ev_tk[SL - 1])
    waitt("sync", M_lerp1_v)
    waitt("sync", M_lerp1_g)
    x1_sp_tk = [None] * SL
    o_fold_v = [None] * SL
    o_fold_a = [None] * SL
    o_ev_tk = [None] * SL
    for s in range(SL):
        st, base = emit_mm_unit(("o", s, "o"), m4)
        if s >= 2:
            waitt("sync", o_fold_v[s - 2])
            wait("sync", s_a, o_fold_a[s - 2])
        tkx, b = xc_dma(xTr[s])
        waitt("vector", tkx)
        if s >= 2:
            waitt("vector", x1_sp_tk[s - 2])
            wait("vector", s_a, o_fold_a[s - 2])
        drain("vector")
        for n in range(NN):
            wait("vector", s_mm, base + n + 1)
            nw = slice(n * NW, (n + 1) * NW)
            e_dve.bump()
            op("vector", lambda v, st=st, n=n, b=b, nw=nw: v.tensor_add(
                hc[:, b, nw], ps[st * 3 + n][:, :], xc[:, b, nw]).then_inc(e_dve.h, 1))
        set_unit_evac(st, e_dve.ticket())
        o_ev_tk[s] = e_dve.ticket()
        waitt("scalar", o_ev_tk[s])
        x1_sp_tk[s] = p_s1.issue(s)
        op("scalar", lambda sc, s=s, b=b, h=x1_sp_tk[s][0]: sc.dma_start(
            out=x1s[s], in_=hc[:, b, :]).then_inc(h, 16))
        if s >= 2:
            waitt("scalar", o_fold_v[s - 2])
        o_fold_a[s] = s_a.bump()
        op("scalar", lambda sc, b=b: sc.activation(
            sq[b][:, :], hc[:, b, :], AF.Square).then_inc(s_a.h, 1))
        wait("vector", s_a, o_fold_a[s])
        o_fold_v[s] = emit_fold_chunk(s, hc[:, b, :], b)

    bc2 = emit_ln_bcast(UNIT_LN2, o_fold_v[SL - 1])
    ab2_tk = emit_stats_math(UNIT_LN2, bc2)

    def x1_src(i):
        waitt("sync", x1_sp_tk[i])
        return xc_dma(x1s[i])

    waitt("sync", o_fold_v[SL - 1])
    wait("sync", s_a, o_fold_a[SL - 1])
    waitt("vector", x1_sp_tk[SL - 2])
    waitt("vector", x1_sp_tk[SL - 1])
    wait("vector", s_a, o_fold_a[SL - 1])
    waitt("gpsimd", x1_sp_tk[SL - 2])
    waitt("gpsimd", x1_sp_tk[SL - 1])
    wait("gpsimd", s_a, o_fold_a[SL - 1])
    M_lerp2_v, M_lerp2_g = emit_apply_lerps(
        x1_src, vec["g2"], vec["b2"],
        [(vec["cmur"], m1), (vec["cmuk"], m2)],
        gp_pre=[o_fold_v[SL - 1], ab2_tk])

    wf_tk = [None] * SL
    tern_tk = [None] * SL
    wt_sp = [None] * SL
    wait("sync", s_mm, mm_cum[mm_idx_of[("o", SL - 1, "o")]])
    for s0 in (0, 1):
        wf_tk[s0] = p_wf.issue(s0)
        op("sync", lambda e, s0=s0, h=wf_tk[s0][0]: e.dma_start(
            out=wfst[:, s0, :, :], in_=wtf[s0]).then_inc(h, 16))

    def emit_tern(s):
        bw = s % 2
        wfv = wfst[:, bw, :, :].rearrange("p k q -> p (k q)")
        wtv = w_t[:, bw, :, :].rearrange("p k q -> p (k q)")
        waitt("vector", wf_tk[s])
        if s >= 2:
            waitt("vector", wt_sp[s - 2])
        drain("vector")
        op("vector", lambda v, wfv=wfv: v.tensor_scalar(
            tern_scr[:, :], wfv, -1.0, theta[:, 0:1], ALU.mult, ALU.is_gt))
        drain("vector")
        tern_tk[s] = s_v.bump()
        op("vector", lambda v, wfv=wfv, wtv=wtv: v.scalar_tensor_tensor(
            wtv, wfv, theta[:, 0:1], tern_scr[:, :],
            ALU.is_gt, ALU.subtract).then_inc(s_v.h, 1))
        wait("sync", s_v, tern_tk[s])
        if s + 2 < SL:
            wf_tk[s + 2] = p_wf.issue(s + 2)
            op("sync", lambda e, s=s, h=wf_tk[s + 2][0]: e.dma_start(
                out=wfst[:, s % 2, :, :], in_=wtf[s + 2]).then_inc(h, 16))
        wt_sp[s] = p_wt.issue(s)
        op("sync", lambda e, s=s, bw=bw, h=wt_sp[s][0]: e.dma_start(
            out=wts[s], in_=w_t[:, bw, :, :]).then_inc(h, 16))
        extra_load_wait[("f8", s, 0)] = wt_sp[s]
        extra_load_wait[("f8", s, 1)] = wt_sp[s]

    waitt("tensor", M_lerp2_v)
    waitt("tensor", M_lerp2_g)
    ck_last_tk = None
    for s in range(SL):
        st, base = emit_mm_unit(("ck", s, "ck"), m2)
        for n in range(NN):
            wait("scalar", s_mm, base + n + 1)
            nw = slice(n * NW, (n + 1) * NW)
            drain("scalar")
            op("scalar", lambda sc, st=st, n=n: sc.activation(
                rl_scr[:, n % 2, :], ps[st * 3 + n][:, :], AF.Relu))
            drain("scalar")
            e_act.bump()
            op("scalar", lambda sc, s=s, n=n, nw=nw: sc.activation(
                m3[:, s, nw], rl_scr[:, n % 2, :], AF.Square).then_inc(e_act.h, 1))
        set_unit_evac(st, e_act.ticket())
        emit_tern(s)
    ck_last_tk = e_act.ticket()



    x2_sp_tk = [None] * SL
    c_fold_v = [None] * SL
    c_fold_a = [None] * SL
    cv_ev_tk = [None] * SL
    waitt("tensor", ck_last_tk)
    wait("scalar", s_v, tern_tk[SL - 1])
    for s in range(SL):
        b = s % 2
        st, base = emit_mm_unit(("crv", s, "cr"), m1)
        if s >= 2:
            waitt("scalar", cv_ev_tk[s - 2])
        for n in range(NN):
            wait("scalar", s_mm, base + n + 1)
            e_act.bump()
            op("scalar", lambda sc, st=st, n=n, b=b: sc.activation(
                r_sl[:, b, n * NW:(n + 1) * NW], ps[st * 3 + n][:, :],
                AF.Sigmoid).then_inc(e_act.h, 1))
        set_unit_evac(st, e_act.ticket())
        cr_tk = e_act.ticket()
        st, base = emit_mm_unit(("crv", s, "cv"), m3)
        if s >= 2:
            waitt("sync", c_fold_v[s - 2])
            wait("sync", s_a, c_fold_a[s - 2])
        waitt("sync", x1_sp_tk[s])
        tkx, _b2 = xc_dma(x1s[s])
        tk2 = p_sl.issue(s)
        op("sync", lambda e, s=s, b=b, h=tk2[0]: e.dma_start(
            out=s_sl[:, b, :], in_=xTr[s]).then_inc(h, 16))
        waitt("vector", cr_tk)
        waitt("vector", tkx)
        waitt("vector", tk2)
        if s >= 2:
            waitt("vector", x2_sp_tk[s - 2])
            wait("vector", s_a, c_fold_a[s - 2])
        for n in range(NN):
            wait("vector", s_mm, base + n + 1)
            nw = slice(n * NW, (n + 1) * NW)
            if n == 0:
                drain("vector")
            op("vector", lambda v, st=st, n=n, b=b, nw=nw: v.tensor_mul(
                rl_scr[:, n % 2, :], ps[st * 3 + n][:, :], r_sl[:, b, nw]))
            drain("vector")
            op("vector", lambda v, n=n, b=_b2, nw=nw: v.tensor_add(
                hc[:, b, nw], rl_scr[:, n % 2, :], xc[:, b, nw]))
            drain("vector")
            e_dve.bump()
            op("vector", lambda v, n=n, b=b, bb=_b2, nw=nw: v.tensor_add(
                hc[:, bb, nw], hc[:, bb, nw], s_sl[:, b, nw]).then_inc(e_dve.h, 1))
        set_unit_evac(st, e_dve.ticket())
        cv_ev_tk[s] = e_dve.ticket()
        waitt("scalar", cv_ev_tk[s])
        x2_sp_tk[s] = p_s2.issue(s)
        op("scalar", lambda sc, s=s, b=_b2, h=x2_sp_tk[s][0]: sc.dma_start(
            out=x2s[s], in_=hc[:, b, :]).then_inc(h, 16))
        if s >= 2:
            waitt("scalar", c_fold_v[s - 2])
        c_fold_a[s] = s_a.bump()
        op("scalar", lambda sc, b=_b2: sc.activation(
            sq[b][:, :], hc[:, b, :], AF.Square).then_inc(s_a.h, 1))
        wait("vector", s_a, c_fold_a[s])
        c_fold_v[s] = emit_fold_chunk(s, hc[:, _b2, :], _b2)

    bc3 = emit_ln_bcast(UNIT_LN3, c_fold_v[SL - 1])
    ab3_tk = emit_stats_math(UNIT_LN3, bc3)

    HS = [(0, 513), (513, 513)]
    x3_tk_h = [[None] * CH, [None] * CH]
    waitt("sync", c_fold_v[SL - 1])
    wait("sync", s_a, c_fold_a[SL - 1])
    waitt("vector", x2_sp_tk[SL - 2])
    waitt("vector", x2_sp_tk[SL - 1])
    wait("vector", s_a, c_fold_a[SL - 1])

    def emit_ln3_apply(h3, i):
        c0, cw = HS[h3]
        waitt("sync", x2_sp_tk[i])
        j = h3 * CH + i
        if j >= 2:
            jj = j - 2
            wait("sync", s_v, x3_tk_h[jj // CH][jj % CH])
        tkx, b = xc_dma(x2s[i][:, c0:c0 + cw], width=cw)
        waitt("vector", tkx)
        drain("vector")
        op("vector", lambda v, b=b, c0=c0, cw=cw: v.tensor_mul(
            hc[:, b, 0:cw], xc[:, b, 0:cw], alpha[:, c0:c0 + cw]))
        drain("vector")
        op("vector", lambda v, i=i, b=b, c0=c0, cw=cw: v.tensor_add(
            x3f[:, i, c0:c0 + cw], hc[:, b, 0:cw], beta[:, c0:c0 + cw]))
        drain("vector")
        x3_tk_h[h3][i] = s_v.bump()
        op("vector", lambda v, i=i, c0=c0, cw=cw: v.tensor_copy(
            m3[:, i, c0:c0 + cw], x3f[:, i, c0:c0 + cw]).then_inc(s_v.h, 1))

    for i in range(CH):
        emit_ln3_apply(0, i)
    M_x3bf_h0 = (s_v.h, x3_tk_h[0][CH - 1])

    waitt("scalar", M_theta_v)

    def emit_lnh_bcast(ui, hcols, fold_ticket):
        waitt("tensor", last_ev_for_set[0])
        waitt("tensor", last_ev_for_set[1])
        waitt("tensor", fold_ticket)
        base = mm_cum[ui] - 4

        def f(t, hcols=hcols):
            for ii, srcT in enumerate((S, S2)):
                for n, (c0, w) in enumerate(hcols):
                    t.matmul(ps[ii * 2 + n][:, 0:w], ones[:, :],
                             srcT[:, c0:c0 + w],
                             start=True, stop=True).then_inc(s_mm.h, 1)
        op("tensor", f)
        return base

    def emit_stats_math_h(ui, bc_base, hcols):
        for n, (c0, w) in enumerate(hcols):
            wait("scalar", s_mm, bc_base + n + 1)
            wait("vector", s_mm, bc_base + 2 + n + 1)
            nw = slice(c0, c0 + w)
            drain("scalar")
            tka1 = s_a.bump()
            op("scalar", lambda sc, nw=nw, n=n: sc.activation(
                beta[:, nw], ps[n][:, 0:nw.stop - nw.start], AF.Square).then_inc(s_a.h, 1))
            wait("vector", s_a, tka1)
            drain("vector")
            tkv = s_v.bump()
            op("vector", lambda v, nw=nw, n=n: v.scalar_tensor_tensor(
                alpha[:, nw], beta[:, nw], -1.0 / 2048.0, ps[2 + n][:, 0:nw.stop - nw.start],
                ALU.mult, ALU.add).then_inc(s_v.h, 1))
            wait("scalar", s_v, tkv)
            op("scalar", lambda sc, nw=nw: sc.activation(
                alpha[:, nw], alpha[:, nw], AF.Ln,
                bias=_f32(EPS_LN), scale=_f32(1.0 / 2048.0)))
            drain("scalar")
            tka = s_a.bump()
            op("scalar", lambda sc, nw=nw: sc.activation(
                alpha[:, nw], alpha[:, nw], AF.Exp,
                bias=_f32(0.0), scale=_f32(-0.5)).then_inc(s_a.h, 1))
            wait("vector", s_a, tka)
            drain("vector")
            e_dve.bump()
            op("vector", lambda v, nw=nw, n=n: v.scalar_tensor_tensor(
                beta[:, nw], ps[n][:, 0:nw.stop - nw.start], -1.0 / 2048.0, alpha[:, nw],
                ALU.mult, ALU.mult).then_inc(e_dve.h, 1))
        tk = e_dve.ticket()
        set_unit_evac(0, tk)
        set_unit_evac(1, tk)
        return tk

    f8_fold = [[None] * SL, [None] * SL]
    f8_fold_a = [[None] * SL, [None] * SL]
    ab_h = [None, None]
    apply0_tk = [None] * CH
    rl_rd_tk = [None, None]
    for h8 in (0, 1):
        c0h, cwh = HS[h8]
        if h8 == 0:
            waitt("tensor", M_x3bf_h0)
        else:
            waitt("tensor", (s_v.h, x3_tk_h[1][CH - 1]))
        for s in range(SL):
            b = s % 2
            st, base = emit_mm_unit(("f8", s, h8), m3, cols=COLS_H[h8])
            for n, (c0, w) in enumerate(COLS_H[h8]):
                wait("scalar", s_mm, base + n + 1)
                waitt("scalar", rl_rd_tk[n % 2])
                drain("scalar")
                e_act.bump()
                op("scalar", lambda sc, st=st, n=n, s=s, w=w: sc.activation(
                    rl_scr[:, n % 2, 0:w], ps[st * 3 + n][:, 0:w], AF.Identity,
                    bias=vec["fbias"][:, s:s + 1],
                    scale=scale_t[:, 0:1]).then_inc(e_act.h, 1))
                act_tk = e_act.ticket()
                waitt("vector", act_tk)
                e_dve.bump()
                op("vector", lambda v, n=n, s=s, c0=c0, w=w: v.tensor_add(
                    x3f[:, s, c0:c0 + w], rl_scr[:, n % 2, 0:w],
                    x3f[:, s, c0:c0 + w]).then_inc(e_dve.h, 1))
                rl_rd_tk[n % 2] = e_dve.ticket()
            set_unit_evac(st, e_act.ticket())
            f8_ev = e_dve.ticket()
            waitt("scalar", f8_ev)
            if s >= 2:
                waitt("scalar", f8_fold[h8][s - 2])
            f8_fold_a[h8][s] = s_a.bump()
            op("scalar", lambda sc, s=s, b=b, c0=c0h, cw=cwh: sc.activation(
                sq[b][:, c0:c0 + cw], x3f[:, s, c0:c0 + cw],
                AF.Square).then_inc(s_a.h, 1))
            if h8 == 0:
                wait("gpsimd", s_a, f8_fold_a[0][s])
                waitt("gpsimd", f8_ev)
                drain("gpsimd")
                if s == 0:
                    op("gpsimd", lambda g, s=s: g.tensor_copy(
                        S[:, 0:513], x3f[:, s, 0:513]))
                    s_g.bump()
                    op("gpsimd", lambda g, b=b: g.tensor_copy(
                        S2[:, 0:513], sq[b][:, 0:513]).then_inc(s_g.h, 1))
                else:
                    op("gpsimd", lambda g, s=s: g.tensor_add(
                        S[:, 0:513], S[:, 0:513], x3f[:, s, 0:513]))
                    s_g.bump()
                    op("gpsimd", lambda g, b=b: g.tensor_add(
                        S2[:, 0:513], S2[:, 0:513], sq[b][:, 0:513]).then_inc(s_g.h, 1))
                f8_fold[0][s] = (s_g.h, s_g.n)
                emit_ln3_apply(1, s)
            else:
                wait("vector", s_a, f8_fold_a[1][s])
                drain("vector")
                if s == 0:
                    op("vector", lambda v, s=s: v.tensor_copy(
                        S[:, 513:T], x3f[:, s, 513:T]))
                    s_v.bump()
                    op("vector", lambda v, b=b: v.tensor_copy(
                        S2[:, 513:T], sq[b][:, 513:T]).then_inc(s_v.h, 1))
                else:
                    op("vector", lambda v, s=s: v.tensor_add(
                        S[:, 513:T], S[:, 513:T], x3f[:, s, 513:T]))
                    s_v.bump()
                    op("vector", lambda v, b=b: v.tensor_add(
                        S2[:, 513:T], S2[:, 513:T], sq[b][:, 513:T]).then_inc(s_v.h, 1))
                f8_fold[1][s] = (s_v.h, s_v.n)
                i = s
                bb = i % 2
                drain("vector")
                op("vector", lambda v, bb=bb, i=i: v.tensor_mul(
                    s_sl[:, bb, 0:513], x3f[:, i, 0:513], alpha[:, 0:513]))
                drain("vector")
                apply0_tk[i] = s_v.bump()
                op("vector", lambda v, bb=bb, i=i: v.tensor_add(
                    x3f[:, i, 0:513], s_sl[:, bb, 0:513],
                    beta[:, 0:513]).then_inc(s_v.h, 1))
        if h8 == 0:
            bch = emit_lnh_bcast(UNIT_LNH0, COLS_H[0], f8_fold[0][SL - 1])
            ab_h[0] = emit_stats_math_h(UNIT_LNH0, bch, COLS_H[0])

    bch = emit_lnh_bcast(UNIT_LNH1, COLS_H[1], f8_fold[1][SL - 1])
    ab_h[1] = emit_stats_math_h(UNIT_LNH1, bch, COLS_H[1])

    out_tks = []
    for i in range(CH):
        bb = i % 2
        drain("vector")
        op("vector", lambda v, bb=bb, i=i: v.tensor_mul(
            s_sl[:, bb, 513:T], x3f[:, i, 513:T], alpha[:, 513:T]))
        drain("vector")
        ov = s_v.bump()
        op("vector", lambda v, bb=bb, i=i: v.tensor_add(
            x3f[:, i, 513:T], s_sl[:, bb, 513:T],
            beta[:, 513:T]).then_inc(s_v.h, 1))
        oq = "sync" if i % 2 == 0 else "scalar"
        wait(oq, s_v, apply0_tk[i])
        wait(oq, s_v, ov)
        if i >= 4:
            waitt(oq, out_tks[i - 4])
        tko = p_o.issue(i)
        out_tks.append(tko)
        op(oq, lambda e, i=i, h=tko[0]: e.dma_start(
            out=outT[i], in_=x3f[:, i, 2:T]).then_inc(h, 16))
    for j in range(CH - 4, CH):
        waitt("sync", out_tks[j])

    with nc.Block() as block:
        @block.sync
        def _(e):
            for f in E["sync"]:
                f(e)

        @block.scalar
        def _(e):
            for f in E["scalar"]:
                f(e)

        @block.vector
        def _(e):
            for f in E["vector"]:
                f(e)

        @block.tensor
        def _(e):
            for f in E["tensor"]:
                f(e)

        @block.gpsimd
        def _(e):
            for f in E["gpsimd"]:
                f(e)

    return nc


def _tile_w(w):
    w = np.asarray(w, dtype=np.float32)
    return np.ascontiguousarray(w.reshape(SL, 128, CH, 128).transpose(0, 3, 2, 1))


def _tile_v(v):
    v = np.asarray(v, dtype=np.float32)
    return np.ascontiguousarray(v.reshape(CH, 128).T)


def kernel(**inp):
    x = np.asarray(inp["L0_attn_x"], dtype=np.float32)
    weights = {
        "wr": _tile_w(inp["L0_attn_tm_Wr"]),
        "wk": _tile_w(inp["L0_attn_tm_Wk"]),
        "wv": _tile_w(inp["L0_attn_tm_Wv"]),
        "wo": _tile_w(inp["L0_attn_tm_Wout"]),
        "cwr": _tile_w(inp["L0_attn_cm_Wr"]),
        "cwk": _tile_w(inp["L0_attn_cm_Wk"]),
        "cwv": _tile_w(inp["L0_attn_cm_Wv"]),
    }
    wfn = _tile_w(inp["L0_ffn_weight"])
    vecs = {
        "g1": _tile_v(inp["L0_attn_ln1_g"]), "b1": _tile_v(inp["L0_attn_ln1_b"]),
        "g2": _tile_v(inp["L0_attn_ln2_g"]), "b2": _tile_v(inp["L0_attn_ln2_b"]),
        "u": _tile_v(inp["L0_attn_tm_u"]),
        "mur": _tile_v(inp["L0_attn_tm_mu_r"]), "muk": _tile_v(inp["L0_attn_tm_mu_k"]),
        "muv": _tile_v(inp["L0_attn_tm_mu_v"]),
        "cmur": _tile_v(inp["L0_attn_cm_mu_r"]), "cmuk": _tile_v(inp["L0_attn_cm_mu_k"]),
        "fbias": _tile_v(inp["L0_ffn_bias"]),
    }
    masks = [np.zeros((128, 2), np.float32), np.ones((128, 2), np.float32)]

    nc = build_program()
    in_maps = []
    for c in range(8):
        b, half = c // 2, c % 2
        if half == 0:
            xt = np.concatenate([np.zeros((2, D), np.float32), x[b, 0:1024]], axis=0)
        else:
            xt = x[b, 1022:2048]
        m = {"xT": np.ascontiguousarray(xt.T), "mask": masks[half], "wt_ffn": wfn}
        for k, v in weights.items():
            m[f"wt_{k}"] = v
        for k, v in vecs.items():
            m[f"v_{k}"] = v
        in_maps.append(m)

    res = run_bass_kernel_spmd(nc, in_maps, core_ids=list(range(8)))
    out = np.empty((4, 2048, D), dtype=np.float32)
    for c, r in enumerate(res.results):
        b, half = c // 2, c % 2
        o = np.asarray(r["outT"])
        out[b, half * 1024:(half + 1) * 1024, :] = o.reshape(D, 1024).T
    return out


# version 34
# speedup vs baseline: 1.0333x; 1.0333x over previous
import math
import numpy as np
import concourse.bass as bass
import concourse.mybir as mybir
from concourse.bass_utils import run_bass_kernel_spmd

F32 = mybir.dt.float32
BF16 = mybir.dt.bfloat16
AF = mybir.ActivationFunctionType
ALU = mybir.AluOpType
AX = mybir.AxisListType

D = 2048
CH = 16
T = 1026
NW = 342
NN = 3
SL = 16
C_SHIFT = float(-math.log(float(np.float32(1e-8))))
EPS_LN = 1e-5
FFN_EPS = 1e-10

MIX_W = ["wr", "wk", "wv", "wo", "cwk", "cwr", "cwv"]
VECS = ["g1", "b1", "g2", "b2", "u", "mur", "muk", "muv", "cmur", "cmuk", "fbias"]


def _f32(x):
    return np.float32(x)


class Ctr:

    def __init__(self, nc, name):
        self.h = nc.alloc_semaphore(name)
        self.n = 0

    def bump(self, k=1):
        self.n += k
        return self.n

    def ticket(self):
        return (self.h, self.n)


class PairCtr:

    def __init__(self, nc, name, n=2):
        self.n = n
        self.h = [nc.alloc_semaphore(f"{name}{i}") for i in range(n)]
        self.cnt = [0] * n

    def issue(self, idx):
        p = idx % self.n
        self.cnt[p] += 16
        return (self.h[p], self.cnt[p])


def build_program(trunc=None):
    nc = bass.Bass("TRN2", target_bir_lowering=False)

    xT = nc.dram_tensor("xT", [D, T], F32, kind="ExternalInput").ap()
    xTr = xT.rearrange("(k p) t -> k p t", p=128)
    wt = {w: nc.dram_tensor(f"wt_{w}", [SL, 128, CH, 128], F32, kind="ExternalInput").ap()
          for w in MIX_W}
    wtf = nc.dram_tensor("wt_ffn", [SL, 128, CH, 128], F32, kind="ExternalInput").ap()
    vec_d = {v: nc.dram_tensor(f"v_{v}", [128, CH], F32, kind="ExternalInput").ap()
             for v in VECS}
    mask_d = nc.dram_tensor("mask", [128, 2], F32, kind="ExternalInput").ap()
    outT = nc.dram_tensor("outT", [CH, 128, 1024], F32, kind="ExternalOutput").ap()
    x1s = nc.dram_tensor("x1s", [CH, 128, T], F32, kind="Internal").ap()
    x2s = nc.dram_tensor("x2s", [CH, 128, T], F32, kind="Internal").ap()
    wts = nc.dram_tensor("wts", [SL, 128, CH, 128], BF16, kind="Internal").ap()

    B0 = (nc.sbuf_base + 31) // 32 * 32
    off = B0

    def at(name, shape, dtype, o):
        return nc.alloc_sbuf_tensor_at(name, shape, dtype, offset=o)

    offs = {}

    def alloc(name, shape, dtype):
        nonlocal off
        nbytes = int(np.prod(shape[1:])) * mybir.dt.size(dtype)
        offs[name] = off
        t = at(name, shape, dtype, off)
        off += (nbytes + 31) // 32 * 32
        return t

    m1 = alloc("m1", [128, CH, T], BF16)
    m2 = alloc("m2", [128, CH, T], BF16)
    m3 = alloc("m3", [128, CH, T], BF16)
    m4 = alloc("m4", [128, CH, T], BF16)
    x3f = at("x3f", [128, CH, T], F32, offs["m1"])
    w_t = at("w_t", [128, 2, CH, 128], BF16, offs["m4"])
    xc = alloc("xc", [128, 2, T], F32)
    xch = at("xch", [128, 4, 513], F32, offs["xc"])
    hc = alloc("hc", [128, 4, T], F32)
    hcpre = at("hcpre", [128, 2, 8, 128], F32, offs["hc"])
    wfst = at("wfst", [128, 2, CH, 128], F32, offs["m4"] + 8192)
    S = alloc("S", [128, T], F32)
    S2 = alloc("S2", [128, T], F32)
    alpha = alloc("alpha", [128, T], F32)
    beta = alloc("beta", [128, T], F32)
    sq = [at("sq0", [128, T], F32, offs["alpha"]),
          at("sq1", [128, T], F32, offs["beta"])]
    r_sl = alloc("r_sl", [128, 2, T], F32)
    tern_scr = at("tern_scr", [128, 2048], F32, offs["r_sl"])
    s_sl = alloc("s_sl", [128, 2, T], F32)
    wsl = alloc("wsl", [128, 4, CH, 128], BF16)
    rl_scr = alloc("rl_scr", [128, 2, NW], F32)
    vec = {v: alloc(f"t_{v}", [128, CH], F32) for v in VECS}
    ush = alloc("ush", [128, CH], F32)
    ones = alloc("ones", [128, 128], F32)
    mask = alloc("maskt", [128, 2], F32)
    theta = alloc("theta", [128, 1], F32)
    scale_t = alloc("scale_t", [128, 1], F32)
    absacc = alloc("absacc", [128, 32], F32)
    absacc1 = alloc("absacc1", [128, 1], F32)
    assert off <= nc.sbuf_top - 32, f"SBUF overflow: {off} > {nc.sbuf_top}"

    ps = [nc.alloc_psum_tensor(f"ps{i}", [128, NW], F32) for i in range(6)]

    s_x = Ctr(nc, "s_x")
    s_mm = Ctr(nc, "s_mm")
    e_act = Ctr(nc, "e_act")
    e_dve = Ctr(nc, "e_dve")
    s_v = Ctr(nc, "s_v")
    s_a = Ctr(nc, "s_a")
    s_g = Ctr(nc, "s_g")
    p_w = PairCtr(nc, "p_w", 4)
    p_xc = PairCtr(nc, "p_xc")
    p_sl = PairCtr(nc, "p_sl")
    p_pre = PairCtr(nc, "p_pre")
    p_wf = PairCtr(nc, "p_wf")
    p_s1 = PairCtr(nc, "p_s1")
    p_s2 = PairCtr(nc, "p_s2")
    p_o = PairCtr(nc, "p_o")
    p_x9 = PairCtr(nc, "p_x9", 4)
    p_wt = PairCtr(nc, "p_wt")

    E = {k: [] for k in ("sync", "scalar", "vector", "tensor", "gpsimd")}
    nops = [0]

    def op(eng, fn):
        nops[0] += 1
        if trunc is None or nops[0] <= trunc:
            E[eng].append(fn)

    def wait(eng, ctr, val):
        if val > 0:
            op(eng, lambda e, h=ctr.h, v=val: e.wait_ge(h, v))

    def waitt(eng, ticket):
        if ticket is None:
            return
        h, v = ticket
        if v > 0:
            op(eng, lambda e, h=h, v=v: e.wait_ge(h, v))

    def drain(eng):
        op(eng, lambda e: e.drain())

    units = [("ln", 1)]
    units += [("mm", ("rkv", s, w)) for s in range(SL) for w in ("r", "k", "v")]
    units += [("theta", None)]
    units += [("mm", ("o", s, "o")) for s in range(SL)]
    units += [("ln", 2)]
    units += [("mm", ("ck", s, "ck")) for s in range(SL)]
    units += [("mm", ("crv", s, w)) for s in range(SL) for w in ("cr", "cv")]
    units += [("ln", 3)]
    units += [("mm2", ("f8", s, 0)) for s in range(SL)]
    units += [("lnh", 0)]
    units += [("mm2", ("f8", s, 1)) for s in range(SL)]
    units += [("lnh", 1)]

    mm_cum = []
    mm_n = 0
    mm_idx_of = {}
    for ui, (kind, info) in enumerate(units):
        if kind == "mm":
            mm_idx_of[info] = ui
            mm_n += 3
        elif kind == "mm2":
            mm_idx_of[info] = ui
            mm_n += 2
        elif kind == "ln":
            mm_n += 6
        elif kind == "lnh":
            mm_n += 4
        else:
            mm_n += 1
        mm_cum.append(mm_n)

    UNIT_TH = 1 + 48
    UNIT_LN2 = UNIT_TH + 1 + 16
    UNIT_LN3 = UNIT_LN2 + 1 + 16 + 32
    UNIT_LNH0 = UNIT_LN3 + 1 + 16
    UNIT_LNH1 = UNIT_LNH0 + 1 + 16
    assert units[UNIT_LN2] == ("ln", 2) and units[UNIT_TH] == ("theta", None)
    assert units[UNIT_LN3] == ("ln", 3)
    assert units[UNIT_LNH0] == ("lnh", 0) and units[UNIT_LNH1] == ("lnh", 1)

    last_ev_for_set = [None, None]
    mm_parity = [0]

    wload_info = []
    for s in range(SL):
        for w, nm in (("r", "wr"), ("k", "wk"), ("v", "wv")):
            wload_info.append((("rkv", s, w), wt[nm][s]))
    for s in range(SL):
        wload_info.append((("o", s, "o"), wt["wo"][s]))
    for s in range(SL):
        wload_info.append((("ck", s, "ck"), wt["cwk"][s]))
    for s in range(SL):
        for w, nm in (("cr", "cwr"), ("cv", "cwv")):
            wload_info.append((("crv", s, w), wt[nm][s]))
    for h in (0, 1):
        for s in range(SL):
            wload_info.append((("f8", s, h), wts[s]))
    extra_load_wait = {}
    w_ticket = {}
    w_pumped = [0]
    WLOOK = 4

    def pump_loads(until):
        for i in range(w_pumped[0], min(until, len(wload_info))):
            key, srcw = wload_info[i]
            if i >= WLOOK:
                prev_ui = mm_idx_of[wload_info[i - WLOOK][0]]
                wait("gpsimd", s_mm, mm_cum[prev_ui])
            if key in extra_load_wait:
                waitt("gpsimd", extra_load_wait[key])
            b = i % 4
            tk = p_w.issue(i)
            w_ticket[key] = (tk, b)
            op("gpsimd", lambda g, srcw=srcw, b=b, h=tk[0]: g.dma_start(
                out=wsl[:, b, :, :], in_=srcw).then_inc(h, 16))
        w_pumped[0] = max(w_pumped[0], min(until, len(wload_info)))

    load_idx_of = {key: i for i, (key, _) in enumerate(wload_info)}

    COLS3 = [(0, NW), (NW, NW), (2 * NW, NW)]
    COLS_H = [[(0, NW), (NW, 171)], [(513, NW), (513 + NW, 171)]]

    def emit_mm_unit(key, rhs_t, lhs_t=None, lhs_b=None, pre_waits=(), cols=None):
        if cols is None:
            cols = COLS3
        ui = mm_idx_of[key]
        st = mm_parity[0] % 2
        mm_parity[0] += 1
        waitt("tensor", last_ev_for_set[st])
        if lhs_t is None:
            pump_loads(load_idx_of[key] + 1 + WLOOK)
            tk, b = w_ticket[key]
            waitt("tensor", tk)
            lhs_t, lhs_b = wsl, b
        for tkt in pre_waits:
            waitt("tensor", tkt)
        base = mm_cum[ui] - len(cols)

        def f(t, st=st, b=lhs_b, lhs_t=lhs_t, rhs_t=rhs_t, cols=cols):
            for k in range(CH):
                for n, (c0, w) in enumerate(cols):
                    ins = t.matmul(ps[st * 3 + n][:, 0:w], lhs_t[:, b, k, :],
                                   rhs_t[:, k, c0:c0 + w],
                                   start=(k == 0), stop=(k == CH - 1))
                    if k == CH - 1:
                        ins.then_inc(s_mm.h, 1)
        op("tensor", f)
        return st, base

    def set_unit_evac(st, ticket):
        last_ev_for_set[st] = ticket

    def emit_ln_bcast(ui, fold_ticket):
        waitt("tensor", last_ev_for_set[0])
        waitt("tensor", last_ev_for_set[1])
        waitt("tensor", fold_ticket)
        base = mm_cum[ui] - 6

        def f(t):
            for i, srcT in enumerate((S, S2)):
                for n in range(NN):
                    t.matmul(ps[i * 3 + n][:, :], ones[:, :],
                             srcT[:, n * NW:(n + 1) * NW],
                             start=True, stop=True).then_inc(s_mm.h, 1)
        op("tensor", f)
        return base

    def emit_stats_math(ui, bc_base):
        for n in range(NN):
            wait("scalar", s_mm, bc_base + n + 1)
            wait("vector", s_mm, bc_base + 3 + n + 1)
            nw = slice(n * NW, (n + 1) * NW)
            drain("scalar")
            tka1 = s_a.bump()
            op("scalar", lambda sc, nw=nw, n=n: sc.activation(
                beta[:, nw], ps[n][:, :], AF.Square).then_inc(s_a.h, 1))
            wait("vector", s_a, tka1)
            drain("vector")
            tkv = s_v.bump()
            op("vector", lambda v, nw=nw, n=n: v.scalar_tensor_tensor(
                alpha[:, nw], beta[:, nw], -1.0 / 2048.0, ps[3 + n][:, :],
                ALU.mult, ALU.add).then_inc(s_v.h, 1))
            wait("scalar", s_v, tkv)
            op("scalar", lambda sc, nw=nw: sc.activation(
                alpha[:, nw], alpha[:, nw], AF.Ln,
                bias=_f32(EPS_LN), scale=_f32(1.0 / 2048.0)))
            drain("scalar")
            tka = s_a.bump()
            op("scalar", lambda sc, nw=nw: sc.activation(
                alpha[:, nw], alpha[:, nw], AF.Exp,
                bias=_f32(0.0), scale=_f32(-0.5)).then_inc(s_a.h, 1))
            wait("vector", s_a, tka)
            drain("vector")
            e_dve.bump()
            op("vector", lambda v, nw=nw, n=n: v.scalar_tensor_tensor(
                beta[:, nw], ps[n][:, :], -1.0 / 2048.0, alpha[:, nw],
                ALU.mult, ALU.mult).then_inc(e_dve.h, 1))
        tk = e_dve.ticket()
        set_unit_evac(0, tk)
        set_unit_evac(1, tk)
        return tk

    for vname in VECS:
        s_x.bump(16)
        op("sync", lambda e, d=vec_d[vname], t=vec[vname]: e.dma_start(
            out=t[:, :], in_=d).then_inc(s_x.h, 16))
    s_x.bump(16)
    op("sync", lambda e: e.dma_start(out=mask[:, :], in_=mask_d).then_inc(s_x.h, 16))
    vecs_done = s_x.n

    M_ones = s_v.bump()
    op("vector", lambda v: v.memset(ones[:, :], 1.0).then_inc(s_v.h, 1))
    wait("vector", s_x, vecs_done)
    M_ush = s_v.bump()
    op("vector", lambda v: v.tensor_scalar_add(
        ush[:, :], vec["u"][:, :], C_SHIFT).then_inc(s_v.h, 1))

    pump_loads(WLOOK)

    def emit_fold_chunk(idx, src_ap, sq_b, eng="vector", ctr=None):
        if ctr is None:
            ctr = s_v
        drain(eng)
        if idx == 0:
            op(eng, lambda v, src_ap=src_ap: v.tensor_copy(S[:, :], src_ap))
            ctr.bump()
            op(eng, lambda v, sq_b=sq_b, h=ctr.h: v.tensor_copy(
                S2[:, :], sq[sq_b][:, :]).then_inc(h, 1))
        else:
            op(eng, lambda v, src_ap=src_ap: v.tensor_add(S[:, :], S[:, :], src_ap))
            ctr.bump()
            op(eng, lambda v, sq_b=sq_b, h=ctr.h: v.tensor_add(
                S2[:, :], S2[:, :], sq[sq_b][:, :]).then_inc(h, 1))
        return ctr.ticket()

    xc_idx = [0]

    def xc_dma(src_ap, width=T):
        i = xc_idx[0]
        xc_idx[0] += 1
        b = i % 2
        tk = p_xc.issue(i)
        op("sync", lambda e, src_ap=src_ap, b=b, h=tk[0], w=width: e.dma_start(
            out=xc[:, b, 0:w], in_=src_ap).then_inc(h, 16))
        return tk, b

    fold_v, fold_a = [], []
    for i in range(CH):
        if i >= 2:
            waitt("sync", fold_v[i - 2])
            wait("sync", s_a, fold_a[i - 2])
        tkx, b = xc_dma(xTr[i])
        waitt("scalar", tkx)
        if i >= 2:
            waitt("scalar", fold_v[i - 2])
        fold_a.append(s_a.bump())
        op("scalar", lambda sc, b=b: sc.activation(
            sq[b][:, :], xc[:, b, :], AF.Square).then_inc(s_a.h, 1))
        waitt("vector", tkx)
        wait("vector", s_a, fold_a[i])
        fold_v.append(emit_fold_chunk(i, xc[:, b, :], b))

    bc1 = emit_ln_bcast(0, fold_v[-1])
    ab1_tk = emit_stats_math(0, bc1)

    HB = 692

    def emit_apply_lerps(chunk_src, g_t, b_t, lerps, gp_pre=()):
        h_tk = [None] * CH
        dv_tk = [None] * CH
        gp_tk = [None] * CH
        wait("vector", s_x, vecs_done)
        wait("gpsimd", s_x, vecs_done)
        wait("scalar", s_x, vecs_done)
        for t in gp_pre:
            waitt("gpsimd", t)
        for i in range(CH):
            if i >= 2:
                wait("sync", s_g, gp_tk[i - 2])
            tkx, b = chunk_src(i)
            bh = i % 4
            waitt("gpsimd", tkx)
            if i >= 4:
                wait("gpsimd", s_v, dv_tk[i - 4])
            drain("gpsimd")
            op("gpsimd", lambda g, b=b, bh=bh: g.tensor_mul(
                hc[:, bh, :], xc[:, b, :], alpha[:, :]))
            drain("gpsimd")
            qb = s_g.bump()
            op("gpsimd", lambda g, b=b, bh=bh: g.tensor_add(
                hc[:, bh, :], hc[:, bh, :], beta[:, :]).then_inc(s_g.h, 1))
            wait("scalar", s_g, qb)
            op("scalar", lambda sc, bh=bh, i=i: sc.activation(
                hc[:, bh, :], hc[:, bh, :], AF.Identity,
                bias=b_t[:, i:i + 1], scale=g_t[:, i:i + 1]))
            drain("scalar")
            h_tk[i] = s_a.bump()
            op("scalar", lambda sc, bh=bh: sc.activation(
                hc[:, bh, 0:2], hc[:, bh, 0:2], AF.Identity,
                scale=mask[:, 0:1]).then_inc(s_a.h, 1))
            gp_tk[i] = qb
            wait("vector", s_a, h_tk[i])
            drain("vector")
            op("vector", lambda v, b=b, bh=bh: v.tensor_sub(
                s_sl[:, b, 1:T], hc[:, bh, 1:T], hc[:, bh, 0:T - 1]))
            drain("vector")
            for j, (mu_t, dst) in enumerate(lerps):
                op("vector", lambda v, b=b, bh=bh, i=i, mu_t=mu_t, dst=dst:
                   v.scalar_tensor_tensor(
                       dst[:, i, 1:T], s_sl[:, b, 1:T], mu_t[:, i:i + 1],
                       hc[:, bh, 0:T - 1], ALU.mult, ALU.add))
                dv_tk[i] = s_v.bump()
                op("vector", lambda v, bh=bh, i=i, mu_t=mu_t, dst=dst:
                   v.tensor_single_scalar(
                       dst[:, i, 0:1], hc[:, bh, 0:1], mu_t[:, i:i + 1],
                       ALU.mult).then_inc(s_v.h, 1))
        return (s_v.h, dv_tk[CH - 1]), (s_g.h, gp_tk[CH - 1])

    waitt("sync", fold_v[CH - 1])
    wait("sync", s_a, fold_a[CH - 1])
    M_lerp1_v, M_lerp1_g = emit_apply_lerps(
        lambda i: xc_dma(xTr[i]), vec["g1"], vec["b1"],
        [(vec["mur"], m1), (vec["muk"], m2), (vec["muv"], m3)],
        gp_pre=[ab1_tk])

    waitt("sync", M_lerp1_v)
    waitt("sync", M_lerp1_g)
    pre_v = [None] * 32

    def prepass_dma(j):
        if j >= 32:
            return
        if j >= 2:
            wait("sync", s_v, pre_v[j - 2])
        sj, hj = j // 2, j % 2
        tkpre = p_pre.issue(j)
        pre_dma_tk[j] = tkpre
        op("sync", lambda e, sj=sj, hj=hj, b=j % 2, h=tkpre[0]: e.dma_start(
            out=hcpre[:, b, :, :], in_=wtf[sj, :, 8 * hj:8 * hj + 8, :]
        ).then_inc(h, 16))

    def prepass_reduce(j):
        if j >= 32:
            return
        waitt("vector", pre_dma_tk[j])
        drain("vector")
        pre_v[j] = s_v.bump()
        op("vector", lambda v, j=j, b=j % 2: v.reduce_sum(
            absacc[:, j:j + 1],
            hcpre[:, b, :, :].rearrange("p a q -> p (a q)"), AX.X,
            apply_absolute_value=True).then_inc(s_v.h, 1))

    pre_dma_tk = [None] * 32
    prepass_dma(0)
    prepass_dma(1)

    waitt("tensor", M_lerp1_v)
    waitt("tensor", M_lerp1_g)
    wait("scalar", s_v, M_ush)
    a_ev_tk = [None] * SL
    for s in range(SL):
        b = s % 2
        st, base = emit_mm_unit(("rkv", s, "r"), m1)
        if s >= 2:
            waitt("scalar", a_ev_tk[s - 2])
        for n in range(NN):
            wait("scalar", s_mm, base + n + 1)
            e_act.bump()
            op("scalar", lambda sc, st=st, n=n, b=b: sc.activation(
                r_sl[:, b, n * NW:(n + 1) * NW], ps[st * 3 + n][:, :],
                AF.Sigmoid).then_inc(e_act.h, 1))
        set_unit_evac(st, e_act.ticket())
        st, base = emit_mm_unit(("rkv", s, "k"), m2)
        for n in range(NN):
            wait("scalar", s_mm, base + n + 1)
            e_act.bump()
            op("scalar", lambda sc, st=st, n=n, b=b, s=s: sc.activation(
                s_sl[:, b, n * NW:(n + 1) * NW], ps[st * 3 + n][:, :],
                AF.Sigmoid, bias=ush[:, s:s + 1]).then_inc(e_act.h, 1))
        set_unit_evac(st, e_act.ticket())
        rs_tk = e_act.ticket()
        st, base = emit_mm_unit(("rkv", s, "v"), m3)
        waitt("vector", rs_tk)
        for n in range(NN):
            wait("vector", s_mm, base + n + 1)
            nw = slice(n * NW, (n + 1) * NW)
            drain("vector")
            op("vector", lambda v, st=st, n=n, b=b, nw=nw: v.tensor_mul(
                rl_scr[:, n % 2, :], ps[st * 3 + n][:, :], s_sl[:, b, nw]))
            drain("vector")
            e_dve.bump()
            op("vector", lambda v, n=n, b=b, s=s, nw=nw: v.tensor_mul(
                m4[:, s, nw], rl_scr[:, n % 2, :],
                r_sl[:, b, nw]).then_inc(e_dve.h, 1))
        set_unit_evac(st, e_dve.ticket())
        a_ev_tk[s] = e_dve.ticket()
        prepass_reduce(2 * s)
        prepass_reduce(2 * s + 1)
        prepass_dma(2 * s + 2)
        prepass_dma(2 * s + 3)

    drain("vector")
    M_absfold = s_v.bump()
    op("vector", lambda v: v.reduce_sum(
        absacc1[:, :], absacc[:, :], AX.X,
        apply_absolute_value=False).then_inc(s_v.h, 1))

    waitt("tensor", last_ev_for_set[0])
    wait("tensor", s_v, M_absfold)
    th_base = mm_cum[UNIT_TH] - 1
    op("tensor", lambda t: t.matmul(
        ps[0][:, 0:1], ones[:, :], absacc1[:, :],
        start=True, stop=True).then_inc(s_mm.h, 1))
    wait("vector", s_mm, th_base + 1)
    drain("vector")
    op("vector", lambda v: v.tensor_scalar_mul(
        scale_t[:, :], ps[0][:, 0:1], 1.0 / 4194304.0))
    drain("vector")
    e_dve.bump()
    op("vector", lambda v: v.tensor_scalar(
        theta[:, :], scale_t[:, :], 0.5, 0.5 * FFN_EPS, ALU.mult, ALU.add
    ).then_inc(e_dve.h, 1))
    set_unit_evac(0, e_dve.ticket())
    M_theta_v = e_dve.ticket()

    waitt("tensor", a_ev_tk[SL - 1])
    waitt("sync", M_lerp1_v)
    waitt("sync", M_lerp1_g)
    x1_sp_tk = [None] * SL
    o_fold_v = [None] * SL
    o_fold_a = [None] * SL
    o_ev_tk = [None] * SL
    for s in range(SL):
        st, base = emit_mm_unit(("o", s, "o"), m4)
        if s >= 2:
            waitt("sync", o_fold_v[s - 2])
            wait("sync", s_a, o_fold_a[s - 2])
        tkx, b = xc_dma(xTr[s])
        waitt("vector", tkx)
        if s >= 2:
            waitt("vector", x1_sp_tk[s - 2])
            wait("vector", s_a, o_fold_a[s - 2])
        drain("vector")
        for n in range(NN):
            wait("vector", s_mm, base + n + 1)
            nw = slice(n * NW, (n + 1) * NW)
            e_dve.bump()
            op("vector", lambda v, st=st, n=n, b=b, nw=nw: v.tensor_add(
                hc[:, b, nw], ps[st * 3 + n][:, :], xc[:, b, nw]).then_inc(e_dve.h, 1))
        set_unit_evac(st, e_dve.ticket())
        o_ev_tk[s] = e_dve.ticket()
        waitt("scalar", o_ev_tk[s])
        x1_sp_tk[s] = p_s1.issue(s)
        op("scalar", lambda sc, s=s, b=b, h=x1_sp_tk[s][0]: sc.dma_start(
            out=x1s[s], in_=hc[:, b, :]).then_inc(h, 16))
        if s >= 2:
            waitt("scalar", o_fold_v[s - 2])
        o_fold_a[s] = s_a.bump()
        op("scalar", lambda sc, b=b: sc.activation(
            sq[b][:, :], hc[:, b, :], AF.Square).then_inc(s_a.h, 1))
        wait("vector", s_a, o_fold_a[s])
        o_fold_v[s] = emit_fold_chunk(s, hc[:, b, :], b)

    bc2 = emit_ln_bcast(UNIT_LN2, o_fold_v[SL - 1])
    ab2_tk = emit_stats_math(UNIT_LN2, bc2)

    def x1_src(i):
        waitt("sync", x1_sp_tk[i])
        return xc_dma(x1s[i])

    waitt("sync", o_fold_v[SL - 1])
    wait("sync", s_a, o_fold_a[SL - 1])
    waitt("vector", x1_sp_tk[SL - 2])
    waitt("vector", x1_sp_tk[SL - 1])
    wait("vector", s_a, o_fold_a[SL - 1])
    waitt("gpsimd", x1_sp_tk[SL - 2])
    waitt("gpsimd", x1_sp_tk[SL - 1])
    wait("gpsimd", s_a, o_fold_a[SL - 1])
    M_lerp2_v, M_lerp2_g = emit_apply_lerps(
        x1_src, vec["g2"], vec["b2"],
        [(vec["cmur"], m1), (vec["cmuk"], m2)],
        gp_pre=[o_fold_v[SL - 1], ab2_tk])

    wf_tk = [None] * SL
    tern_tk = [None] * SL
    wt_sp = [None] * SL
    wait("sync", s_mm, mm_cum[mm_idx_of[("o", SL - 1, "o")]])
    for s0 in (0, 1):
        wf_tk[s0] = p_wf.issue(s0)
        op("sync", lambda e, s0=s0, h=wf_tk[s0][0]: e.dma_start(
            out=wfst[:, s0, :, :], in_=wtf[s0]).then_inc(h, 16))

    def emit_tern(s):
        bw = s % 2
        wfv = wfst[:, bw, :, :].rearrange("p k q -> p (k q)")
        wtv = w_t[:, bw, :, :].rearrange("p k q -> p (k q)")
        waitt("vector", wf_tk[s])
        if s >= 2:
            waitt("vector", wt_sp[s - 2])
        drain("vector")
        op("vector", lambda v, wfv=wfv: v.tensor_scalar(
            tern_scr[:, :], wfv, -1.0, theta[:, 0:1], ALU.mult, ALU.is_gt))
        drain("vector")
        tern_tk[s] = s_v.bump()
        op("vector", lambda v, wfv=wfv, wtv=wtv: v.scalar_tensor_tensor(
            wtv, wfv, theta[:, 0:1], tern_scr[:, :],
            ALU.is_gt, ALU.subtract).then_inc(s_v.h, 1))
        wait("sync", s_v, tern_tk[s])
        if s + 2 < SL:
            wf_tk[s + 2] = p_wf.issue(s + 2)
            op("sync", lambda e, s=s, h=wf_tk[s + 2][0]: e.dma_start(
                out=wfst[:, s % 2, :, :], in_=wtf[s + 2]).then_inc(h, 16))
        wt_sp[s] = p_wt.issue(s)
        op("sync", lambda e, s=s, bw=bw, h=wt_sp[s][0]: e.dma_start(
            out=wts[s], in_=w_t[:, bw, :, :]).then_inc(h, 16))
        extra_load_wait[("f8", s, 0)] = wt_sp[s]
        extra_load_wait[("f8", s, 1)] = wt_sp[s]

    waitt("tensor", M_lerp2_v)
    waitt("tensor", M_lerp2_g)
    ck_last_tk = None
    for s in range(SL):
        st, base = emit_mm_unit(("ck", s, "ck"), m2)
        for n in range(NN):
            wait("scalar", s_mm, base + n + 1)
            nw = slice(n * NW, (n + 1) * NW)
            drain("scalar")
            op("scalar", lambda sc, st=st, n=n: sc.activation(
                rl_scr[:, n % 2, :], ps[st * 3 + n][:, :], AF.Relu))
            drain("scalar")
            e_act.bump()
            op("scalar", lambda sc, s=s, n=n, nw=nw: sc.activation(
                m3[:, s, nw], rl_scr[:, n % 2, :], AF.Square).then_inc(e_act.h, 1))
        set_unit_evac(st, e_act.ticket())
        emit_tern(s)
    ck_last_tk = e_act.ticket()



    x2_sp_tk = [None] * SL
    c_fold_v = [None] * SL
    c_fold_a = [None] * SL
    cv_ev_tk = [None] * SL
    waitt("tensor", ck_last_tk)
    wait("scalar", s_v, tern_tk[SL - 1])
    for s in range(SL):
        b = s % 2
        st, base = emit_mm_unit(("crv", s, "cr"), m1)
        if s >= 2:
            waitt("scalar", cv_ev_tk[s - 2])
        for n in range(NN):
            wait("scalar", s_mm, base + n + 1)
            e_act.bump()
            op("scalar", lambda sc, st=st, n=n, b=b: sc.activation(
                r_sl[:, b, n * NW:(n + 1) * NW], ps[st * 3 + n][:, :],
                AF.Sigmoid).then_inc(e_act.h, 1))
        set_unit_evac(st, e_act.ticket())
        cr_tk = e_act.ticket()
        st, base = emit_mm_unit(("crv", s, "cv"), m3)
        if s >= 2:
            waitt("sync", c_fold_v[s - 2])
            wait("sync", s_a, c_fold_a[s - 2])
        waitt("sync", x1_sp_tk[s])
        tkx, _b2 = xc_dma(x1s[s])
        tk2 = p_sl.issue(s)
        op("sync", lambda e, s=s, b=b, h=tk2[0]: e.dma_start(
            out=s_sl[:, b, :], in_=xTr[s]).then_inc(h, 16))
        waitt("vector", cr_tk)
        waitt("vector", tkx)
        waitt("vector", tk2)
        if s >= 2:
            waitt("vector", x2_sp_tk[s - 2])
            wait("vector", s_a, c_fold_a[s - 2])
        for n in range(NN):
            wait("vector", s_mm, base + n + 1)
            nw = slice(n * NW, (n + 1) * NW)
            if n == 0:
                drain("vector")
            op("vector", lambda v, st=st, n=n, b=b, nw=nw: v.tensor_mul(
                rl_scr[:, n % 2, :], ps[st * 3 + n][:, :], r_sl[:, b, nw]))
            drain("vector")
            op("vector", lambda v, n=n, b=_b2, nw=nw: v.tensor_add(
                hc[:, b, nw], rl_scr[:, n % 2, :], xc[:, b, nw]))
            drain("vector")
            e_dve.bump()
            op("vector", lambda v, n=n, b=b, bb=_b2, nw=nw: v.tensor_add(
                hc[:, bb, nw], hc[:, bb, nw], s_sl[:, b, nw]).then_inc(e_dve.h, 1))
        set_unit_evac(st, e_dve.ticket())
        cv_ev_tk[s] = e_dve.ticket()
        waitt("scalar", cv_ev_tk[s])
        x2_sp_tk[s] = p_s2.issue(s)
        op("scalar", lambda sc, s=s, b=_b2, h=x2_sp_tk[s][0]: sc.dma_start(
            out=x2s[s], in_=hc[:, b, :]).then_inc(h, 16))
        if s >= 2:
            waitt("scalar", c_fold_v[s - 2])
        c_fold_a[s] = s_a.bump()
        op("scalar", lambda sc, b=_b2: sc.activation(
            sq[b][:, :], hc[:, b, :], AF.Square).then_inc(s_a.h, 1))
        wait("vector", s_a, c_fold_a[s])
        c_fold_v[s] = emit_fold_chunk(s, hc[:, _b2, :], _b2)

    bc3 = emit_ln_bcast(UNIT_LN3, c_fold_v[SL - 1])
    ab3_tk = emit_stats_math(UNIT_LN3, bc3)

    HS = [(0, 513), (513, 513)]
    x3_tk_h = [[None] * CH, [None] * CH]
    waitt("sync", c_fold_v[SL - 1])
    wait("sync", s_a, c_fold_a[SL - 1])
    waitt("vector", x2_sp_tk[SL - 2])
    waitt("vector", x2_sp_tk[SL - 1])
    wait("vector", s_a, c_fold_a[SL - 1])

    def emit_ln3_apply(h3, i):
        c0, cw = HS[h3]
        waitt("sync", x2_sp_tk[i])
        j = h3 * CH + i
        if j >= 4:
            jj = j - 4
            wait("sync", s_v, x3_tk_h[jj // CH][jj % CH])
        b = j % 4
        tkx = p_x9.issue(j)
        op("sync", lambda e, i=i, b=b, c0=c0, cw=cw, h=tkx[0]: e.dma_start(
            out=xch[:, b, 0:cw], in_=x2s[i][:, c0:c0 + cw]).then_inc(h, 16))
        waitt("vector", tkx)
        drain("vector")
        op("vector", lambda v, b=b, c0=c0, cw=cw: v.tensor_mul(
            hc[:, b % 2, 0:cw], xch[:, b, 0:cw], alpha[:, c0:c0 + cw]))
        drain("vector")
        op("vector", lambda v, i=i, b=b, c0=c0, cw=cw: v.tensor_add(
            x3f[:, i, c0:c0 + cw], hc[:, b % 2, 0:cw], beta[:, c0:c0 + cw]))
        drain("vector")
        x3_tk_h[h3][i] = s_v.bump()
        op("vector", lambda v, i=i, c0=c0, cw=cw: v.tensor_copy(
            m3[:, i, c0:c0 + cw], x3f[:, i, c0:c0 + cw]).then_inc(s_v.h, 1))

    for i in range(CH):
        emit_ln3_apply(0, i)
    M_x3bf_h0 = (s_v.h, x3_tk_h[0][CH - 1])

    waitt("scalar", M_theta_v)

    def emit_lnh_bcast(ui, hcols, fold_ticket):
        waitt("tensor", last_ev_for_set[0])
        waitt("tensor", last_ev_for_set[1])
        waitt("tensor", fold_ticket)
        base = mm_cum[ui] - 4

        def f(t, hcols=hcols):
            for ii, srcT in enumerate((S, S2)):
                for n, (c0, w) in enumerate(hcols):
                    t.matmul(ps[ii * 2 + n][:, 0:w], ones[:, :],
                             srcT[:, c0:c0 + w],
                             start=True, stop=True).then_inc(s_mm.h, 1)
        op("tensor", f)
        return base

    def emit_stats_math_h(ui, bc_base, hcols):
        for n, (c0, w) in enumerate(hcols):
            wait("scalar", s_mm, bc_base + n + 1)
            wait("vector", s_mm, bc_base + 2 + n + 1)
            nw = slice(c0, c0 + w)
            drain("scalar")
            tka1 = s_a.bump()
            op("scalar", lambda sc, nw=nw, n=n: sc.activation(
                beta[:, nw], ps[n][:, 0:nw.stop - nw.start], AF.Square).then_inc(s_a.h, 1))
            wait("vector", s_a, tka1)
            drain("vector")
            tkv = s_v.bump()
            op("vector", lambda v, nw=nw, n=n: v.scalar_tensor_tensor(
                alpha[:, nw], beta[:, nw], -1.0 / 2048.0, ps[2 + n][:, 0:nw.stop - nw.start],
                ALU.mult, ALU.add).then_inc(s_v.h, 1))
            wait("scalar", s_v, tkv)
            op("scalar", lambda sc, nw=nw: sc.activation(
                alpha[:, nw], alpha[:, nw], AF.Ln,
                bias=_f32(EPS_LN), scale=_f32(1.0 / 2048.0)))
            drain("scalar")
            tka = s_a.bump()
            op("scalar", lambda sc, nw=nw: sc.activation(
                alpha[:, nw], alpha[:, nw], AF.Exp,
                bias=_f32(0.0), scale=_f32(-0.5)).then_inc(s_a.h, 1))
            wait("vector", s_a, tka)
            drain("vector")
            e_dve.bump()
            op("vector", lambda v, nw=nw, n=n: v.scalar_tensor_tensor(
                beta[:, nw], ps[n][:, 0:nw.stop - nw.start], -1.0 / 2048.0, alpha[:, nw],
                ALU.mult, ALU.mult).then_inc(e_dve.h, 1))
        tk = e_dve.ticket()
        set_unit_evac(0, tk)
        set_unit_evac(1, tk)
        return tk

    f8_fold = [[None] * SL, [None] * SL]
    f8_fold_a = [[None] * SL, [None] * SL]
    ab_h = [None, None]
    apply0_tk = [None] * CH
    rl_rd_tk = [None, None]
    for h8 in (0, 1):
        c0h, cwh = HS[h8]
        if h8 == 0:
            waitt("tensor", M_x3bf_h0)
        else:
            waitt("tensor", (s_v.h, x3_tk_h[1][CH - 1]))
        for s in range(SL):
            b = s % 2
            st, base = emit_mm_unit(("f8", s, h8), m3, cols=COLS_H[h8])
            for n, (c0, w) in enumerate(COLS_H[h8]):
                wait("scalar", s_mm, base + n + 1)
                waitt("scalar", rl_rd_tk[n % 2])
                drain("scalar")
                e_act.bump()
                op("scalar", lambda sc, st=st, n=n, s=s, w=w: sc.activation(
                    rl_scr[:, n % 2, 0:w], ps[st * 3 + n][:, 0:w], AF.Identity,
                    bias=vec["fbias"][:, s:s + 1],
                    scale=scale_t[:, 0:1]).then_inc(e_act.h, 1))
                act_tk = e_act.ticket()
                waitt("vector", act_tk)
                e_dve.bump()
                op("vector", lambda v, n=n, s=s, c0=c0, w=w: v.tensor_add(
                    x3f[:, s, c0:c0 + w], rl_scr[:, n % 2, 0:w],
                    x3f[:, s, c0:c0 + w]).then_inc(e_dve.h, 1))
                rl_rd_tk[n % 2] = e_dve.ticket()
            set_unit_evac(st, e_act.ticket())
            f8_ev = e_dve.ticket()
            waitt("scalar", f8_ev)
            if s >= 2:
                waitt("scalar", f8_fold[h8][s - 2])
            f8_fold_a[h8][s] = s_a.bump()
            op("scalar", lambda sc, s=s, b=b, c0=c0h, cw=cwh: sc.activation(
                sq[b][:, c0:c0 + cw], x3f[:, s, c0:c0 + cw],
                AF.Square).then_inc(s_a.h, 1))
            if h8 == 0:
                wait("gpsimd", s_a, f8_fold_a[0][s])
                waitt("gpsimd", f8_ev)
                drain("gpsimd")
                if s == 0:
                    op("gpsimd", lambda g, s=s: g.tensor_copy(
                        S[:, 0:513], x3f[:, s, 0:513]))
                    s_g.bump()
                    op("gpsimd", lambda g, b=b: g.tensor_copy(
                        S2[:, 0:513], sq[b][:, 0:513]).then_inc(s_g.h, 1))
                else:
                    op("gpsimd", lambda g, s=s: g.tensor_add(
                        S[:, 0:513], S[:, 0:513], x3f[:, s, 0:513]))
                    s_g.bump()
                    op("gpsimd", lambda g, b=b: g.tensor_add(
                        S2[:, 0:513], S2[:, 0:513], sq[b][:, 0:513]).then_inc(s_g.h, 1))
                f8_fold[0][s] = (s_g.h, s_g.n)
                emit_ln3_apply(1, s)
            else:
                wait("vector", s_a, f8_fold_a[1][s])
                drain("vector")
                if s == 0:
                    op("vector", lambda v, s=s: v.tensor_copy(
                        S[:, 513:T], x3f[:, s, 513:T]))
                    s_v.bump()
                    op("vector", lambda v, b=b: v.tensor_copy(
                        S2[:, 513:T], sq[b][:, 513:T]).then_inc(s_v.h, 1))
                else:
                    op("vector", lambda v, s=s: v.tensor_add(
                        S[:, 513:T], S[:, 513:T], x3f[:, s, 513:T]))
                    s_v.bump()
                    op("vector", lambda v, b=b: v.tensor_add(
                        S2[:, 513:T], S2[:, 513:T], sq[b][:, 513:T]).then_inc(s_v.h, 1))
                f8_fold[1][s] = (s_v.h, s_v.n)
                i = s
                bb = i % 2
                drain("vector")
                op("vector", lambda v, bb=bb, i=i: v.tensor_mul(
                    s_sl[:, bb, 0:513], x3f[:, i, 0:513], alpha[:, 0:513]))
                drain("vector")
                apply0_tk[i] = s_v.bump()
                op("vector", lambda v, bb=bb, i=i: v.tensor_add(
                    x3f[:, i, 0:513], s_sl[:, bb, 0:513],
                    beta[:, 0:513]).then_inc(s_v.h, 1))
        if h8 == 0:
            bch = emit_lnh_bcast(UNIT_LNH0, COLS_H[0], f8_fold[0][SL - 1])
            ab_h[0] = emit_stats_math_h(UNIT_LNH0, bch, COLS_H[0])

    bch = emit_lnh_bcast(UNIT_LNH1, COLS_H[1], f8_fold[1][SL - 1])
    ab_h[1] = emit_stats_math_h(UNIT_LNH1, bch, COLS_H[1])

    out_tks = []
    for i in range(CH):
        bb = i % 2
        drain("vector")
        op("vector", lambda v, bb=bb, i=i: v.tensor_mul(
            s_sl[:, bb, 513:T], x3f[:, i, 513:T], alpha[:, 513:T]))
        drain("vector")
        ov = s_v.bump()
        op("vector", lambda v, bb=bb, i=i: v.tensor_add(
            x3f[:, i, 513:T], s_sl[:, bb, 513:T],
            beta[:, 513:T]).then_inc(s_v.h, 1))
        wait("sync", s_v, apply0_tk[i])
        wait("sync", s_v, ov)
        tko = p_o.issue(i)
        out_tks.append(tko)
        op("sync", lambda e, i=i, h=tko[0]: e.dma_start(
            out=outT[i], in_=x3f[:, i, 2:T]).then_inc(h, 16))
    waitt("sync", out_tks[CH - 2])
    waitt("sync", out_tks[CH - 1])

    with nc.Block() as block:
        @block.sync
        def _(e):
            for f in E["sync"]:
                f(e)

        @block.scalar
        def _(e):
            for f in E["scalar"]:
                f(e)

        @block.vector
        def _(e):
            for f in E["vector"]:
                f(e)

        @block.tensor
        def _(e):
            for f in E["tensor"]:
                f(e)

        @block.gpsimd
        def _(e):
            for f in E["gpsimd"]:
                f(e)

    return nc


def _tile_w(w):
    w = np.asarray(w, dtype=np.float32)
    return np.ascontiguousarray(w.reshape(SL, 128, CH, 128).transpose(0, 3, 2, 1))


def _tile_v(v):
    v = np.asarray(v, dtype=np.float32)
    return np.ascontiguousarray(v.reshape(CH, 128).T)


def kernel(**inp):
    x = np.asarray(inp["L0_attn_x"], dtype=np.float32)
    weights = {
        "wr": _tile_w(inp["L0_attn_tm_Wr"]),
        "wk": _tile_w(inp["L0_attn_tm_Wk"]),
        "wv": _tile_w(inp["L0_attn_tm_Wv"]),
        "wo": _tile_w(inp["L0_attn_tm_Wout"]),
        "cwr": _tile_w(inp["L0_attn_cm_Wr"]),
        "cwk": _tile_w(inp["L0_attn_cm_Wk"]),
        "cwv": _tile_w(inp["L0_attn_cm_Wv"]),
    }
    wfn = _tile_w(inp["L0_ffn_weight"])
    vecs = {
        "g1": _tile_v(inp["L0_attn_ln1_g"]), "b1": _tile_v(inp["L0_attn_ln1_b"]),
        "g2": _tile_v(inp["L0_attn_ln2_g"]), "b2": _tile_v(inp["L0_attn_ln2_b"]),
        "u": _tile_v(inp["L0_attn_tm_u"]),
        "mur": _tile_v(inp["L0_attn_tm_mu_r"]), "muk": _tile_v(inp["L0_attn_tm_mu_k"]),
        "muv": _tile_v(inp["L0_attn_tm_mu_v"]),
        "cmur": _tile_v(inp["L0_attn_cm_mu_r"]), "cmuk": _tile_v(inp["L0_attn_cm_mu_k"]),
        "fbias": _tile_v(inp["L0_ffn_bias"]),
    }
    masks = [np.zeros((128, 2), np.float32), np.ones((128, 2), np.float32)]

    nc = build_program()
    in_maps = []
    for c in range(8):
        b, half = c // 2, c % 2
        if half == 0:
            xt = np.concatenate([np.zeros((2, D), np.float32), x[b, 0:1024]], axis=0)
        else:
            xt = x[b, 1022:2048]
        m = {"xT": np.ascontiguousarray(xt.T), "mask": masks[half], "wt_ffn": wfn}
        for k, v in weights.items():
            m[f"wt_{k}"] = v
        for k, v in vecs.items():
            m[f"v_{k}"] = v
        in_maps.append(m)

    res = run_bass_kernel_spmd(nc, in_maps, core_ids=list(range(8)))
    out = np.empty((4, 2048, D), dtype=np.float32)
    for c, r in enumerate(res.results):
        b, half = c // 2, c % 2
        o = np.asarray(r["outT"])
        out[b, half * 1024:(half + 1) * 1024, :] = o.reshape(D, 1024).T
    return out
